# version 01 (final)
# baseline (speedup 1.0000x reference)
import numpy as np
import ml_dtypes
import concourse.bass as bass
import concourse.mybir as mybir
from concourse.bass_utils import run_bass_kernel_spmd
from concourse.bass_core_rust import IndirectOffsetOnAxis

F32 = mybir.dt.float32
BF16 = mybir.dt.bfloat16
I32 = mybir.dt.int32
AF = mybir.ActivationFunctionType
AL = mybir.AluOpType

B = 4
T = 4096
D = 2048
DC = D // 128
NHC = 16
HD = 64
CK = 16
WND = 128
CPW = WND // CK
NBLK = 4
BLKT = T // NBLK
WPB = BLKT // WND
FH = 16
FHD = 128
FFN = 8192
SEL = 1024
SELH = SEL // 2
EPS_LN = 1e-5
EPS_TTT = 1e-6
EPS_RMS = 1e-6
RG = [[0, 1, 2, 3, 4, 5, 6, 7]]
RGP = [[0, 1], [2, 3], [4, 5], [6, 7]]
TH = T // 2


class Prog:
    ENGINES = ("tensor", "vector", "scalar", "gpsimd", "sync")

    def __init__(self, nc):
        self.nc = nc
        self.streams = {e: [] for e in self.ENGINES}
        self.sems = {}
        self.sem_counts = {}
        self._exits = []

    def sem(self, name):
        if name not in self.sems:
            cm = self.nc.semaphore(name)
            self.sems[name] = cm.__enter__()
            self._exits.append(cm)
            self.sem_counts[name] = 0
        return self.sems[name]

    def bump(self, name, n=1):
        self.sem(name)
        self.sem_counts[name] += n
        return self.sem_counts[name]

    def count(self, name):
        return self.sem_counts.get(name, 0)

    def add(self, engine, fn):
        self.streams[engine].append(fn)

    def op(self, engine, make, sem_name=None, inc=1):
        if sem_name is None:
            self.add(engine, lambda eng: make(eng))
            return None
        s = self.sem(sem_name)
        v = self.bump(sem_name, inc)

        def fn(eng):
            make(eng).then_inc(s, inc)
        self.add(engine, fn)
        return v

    def opc(self, engine, make):
        name = "ser_" + engine
        s = self.sem(name)
        v = self.bump(name, 1)

        def fn(eng):
            make(eng).then_inc(s, 1)
            eng.wait_ge(s, v)
        self.add(engine, fn)
        return v

    def opa(self, engine, make):
        name = "ser_" + engine
        s = self.sem(name)
        v = self.bump(name, 1)

        def fn(eng):
            make(eng).then_inc(s, 1)
        self.add(engine, fn)
        return v

    def wait(self, engine, sem_name, val):
        s = self.sem(sem_name)
        self.add(engine, lambda eng: eng.wait_ge(s, val))

    def barrier(self, tag=""):
        vals = {e: self.bump(f"bar_{e}", 1) for e in self.ENGINES}
        sems = {e: self.sems[f"bar_{e}"] for e in self.ENGINES}

        def mk(e):
            def fn(eng):
                eng.sem_inc(sems[e], 1)
                for e2 in self.ENGINES:
                    eng.wait_ge(sems[e2], vals[e2])
            return fn
        for e in self.ENGINES:
            self.add(e, mk(e))

    def emit(self, block):
        for e in self.ENGINES:
            stream = self.streams[e]
            sec = getattr(block, e)

            def body(eng, stream=stream):
                for fn in stream:
                    fn(eng)
            sec(body)



def prepare_shards(inp):
    C = np.ascontiguousarray
    maps = []
    for c in range(8):
        b = c // 2
        half = c % 2
        hs = slice(half * NHC, (half + 1) * NHC)
        lnw = inp["L0_attn_ln_weight"][hs]
        lnb = inp["L0_attn_ln_bias"][hs]
        def fm16(a):
            return C(a.reshape(8, 2, HD).transpose(1, 2, 0).reshape(128, 8))
        def brep(a):
            r = a.reshape(8, 2, HD)[:, None, :, :]
            return C(np.broadcast_to(r, (8, 16, 2, HD)).reshape(128, 2, HD))
        p_ = np.arange(128, dtype=np.int32)[:, None]
        j_ = np.arange(8, dtype=np.int32)[None, :]
        oidx_own = j_ * 256 + p_ * 2 + half
        oidx_snd = j_ * 256 + p_ * 2 + (1 - half)
        oidx_prt = (1 - half) * 1024 + j_ * 128 + p_
        woT = inp["L0_attn_Wo"].T
        m = {
            "x_in": C(inp["L0_attn_hidden_states"][b]),
            "x_own": C(inp["L0_attn_hidden_states"][b, half * TH:(half + 1) * TH]),
            "wq_t": C(inp["L0_attn_Wq"][hs.start * HD:hs.stop * HD].T),
            "wk_t": C(inp["L0_attn_Wk"][hs.start * HD:hs.stop * HD].T),
            "wv_t": C(inp["L0_attn_Wv"][hs.start * HD:hs.stop * HD].T),
            "wo_t": C(np.concatenate([woT[half * 1024:(half + 1) * 1024],
                                      woT[(1 - half) * 1024:(2 - half) * 1024]])),
            "oidx_pack": C(np.concatenate([oidx_own, oidx_snd, oidx_prt], 1)),
            "wrrep": C(np.broadcast_to(inp["L0_ffn_w_router"][None, :], (128, D))),
            "lw_t": C(inp["L0_attn_linear_weight"][hs, 0, :].T),
            "lnw_fm": fm16(lnw), "lnb_fm": fm16(lnb),
            "lnw_tm": C(lnw.reshape(8, 2, HD).reshape(1, 16 * HD)),
            "lnb_tm": C(lnb.reshape(8, 2, HD).reshape(1, 16 * HD)),
            "lb_row": C(inp["L0_attn_linear_bias"][hs, 0].reshape(1, NHC)),
            "lnw_rep": brep(lnw), "lnb_rep": brep(lnb),
            "w1_fm": C(inp["L0_attn_W1_init"][hs].reshape(8, 2, HD, HD)
                       .transpose(1, 2, 0, 3).reshape(128, 8 * HD)),
            "b1_fm": fm16(inp["L0_attn_b1_init"][hs, 0, :].reshape(NHC, HD)),
            "f_mask": C(inp["L0_ffn_mask"]),
            "f_freqs_c": C(inp["L0_ffn_freqs_cis"][:, :, 0].T),
            "f_freqs_s": C(inp["L0_ffn_freqs_cis"][:, :, 1].T),
            "f_router": C(inp["L0_ffn_w_router"].reshape(DC, 128).T),
            "f_g1": C(inp["L0_ffn_g1"].reshape(DC, 128).T),
            "f_g2": C(inp["L0_ffn_g2"].reshape(DC, 128).T),
            "f_wq": C(inp["L0_ffn_wq"]),
            "f_wk": C(inp["L0_ffn_wk"]),
            "f_wv": C(inp["L0_ffn_wv"]),
            "f_wo": C(inp["L0_ffn_wo"]),
            "f_w1": C(inp["L0_ffn_w1"]),
            "f_w3": C(inp["L0_ffn_w3"]),
            "f_w2": C(inp["L0_ffn_w2"]),
            "c_fm_in": C(inp["L0_c"].T.reshape(DC, 128, B).transpose(1, 0, 2)),
            "bsel_in": np.eye(B, dtype=np.float32)[:, b:b + 1].copy(),
            "wmod_sl": C(inp["L0_wmod"][:, c * 1536:(c + 1) * 1536]),
        }
        maps.append(m)
    return maps


def assemble(res):
    out = np.empty((B, T, D), dtype=np.float32)
    for b in range(B):
        out[b, :TH] = res.results[2 * b]["out"]
        out[b, TH:] = res.results[2 * b + 1]["out"]
    return out


def kernel(**inputs):
    maps = prepare_shards(inputs)
    nc = build()
    res = run_bass_kernel_spmd(nc, maps, core_ids=list(range(8)))
    return assemble(res)


def build(stage=None, taps=(), nwin_limit=None):
    nc = bass.Bass("TRN2", target_bir_lowering=False)
    nc.num_devices = 8
    bld = Builder(nc, stage, set(taps))
    bld.nwin_limit = nwin_limit
    with nc.allow_non_contiguous_dma("intentional small strided loads"):
        bld.run()
    return nc


class Builder:
    def __init__(self, nc, stage=None, taps=()):
        self.nc = nc
        self.stage = stage
        self.taps = set(taps)
        self.p = Prog(nc)
        self.dram = {}
        self.sb_h = {}
        self._persist = []
        self._scopes = []

    def din(self, name, shape, dt=F32):
        t = self.nc.dram_tensor(name, list(shape), dt, kind="ExternalInput")
        self.dram[name] = t.ap()
        return self.dram[name]

    def dout(self, name, shape, dt=F32):
        t = self.nc.dram_tensor(name, list(shape), dt, kind="ExternalOutput")
        self.dram[name] = t.ap()
        return self.dram[name]

    def dint(self, name, shape, dt=F32):
        t = self.nc.dram_tensor(name, list(shape), dt, kind="Internal")
        self.dram[name] = t.ap()
        return self.dram[name]

    def sb(self, name, shape, dt=F32):
        cm = self.nc.sbuf_tensor(name, list(shape), dt, side="left")
        h = cm.__enter__()
        self._persist.append(cm)
        self.sb_h[name] = h
        return h

    def sbt(self, name, shape, dt=F32):
        cm = self.nc.sbuf_tensor(name, list(shape), dt, side="right")
        h = cm.__enter__()
        self._scopes[-1].append(cm)
        self.sb_h[name] = h
        return h

    def pst(self, name, shape, dt=F32):
        cm = self.nc.psum_tensor(name, list(shape), dt)
        h = cm.__enter__()
        self._scopes[-1].append(cm)
        return h

    def push(self):
        self._scopes.append([])

    def pop(self):
        for cm in reversed(self._scopes.pop()):
            cm.__exit__(None, None, None)

    def tap_dram(self, name, dram_name, shape, dt=F32):
        if name not in self.taps:
            return
        d = self.dout("tap_" + name, shape, dt)
        p = self.p
        p.barrier()
        self.push()
        flat = int(np.prod(shape))
        W = 2048
        rows = flat // W
        assert rows * W == flat
        src = self.dram[dram_name].flatten().rearrange("(r w) -> r w", w=W)
        dst = d.flatten().rearrange("(r w) -> r w", w=W)
        bounce = self.sbt("tapb_" + name, (128, W), dt)
        for i in range(0, rows, 128):
            r = min(128, rows - i)
            v = p.op("gpsimd", lambda g, i=i, r=r: g.dma_start(
                out=bounce[:r, :], in_=src[i:i + r]), "tapsem", 16)
            p.wait("gpsimd", "tapsem", v)
            v = p.op("gpsimd", lambda g, i=i, r=r: g.dma_start(
                out=dst[i:i + r], in_=bounce[:r, :]), "tapsem", 16)
            p.wait("gpsimd", "tapsem", v)
        self.pop()

    def run(self):
        self.declare_io()
        self.phase_consts()
        self.phase_mod()
        self.phase_ln1()
        if self.stage == "h1":
            return self.finish()
        self.phase_wprep()
        self.phase_proj_scan()
        if self.stage in ("proj", "scan", "scan_pre", "scan_chunks", "scan_chain", "scan_ld", "scan_bd", "scan_xbtm", "scan_gram", "scan_gmm", "scan_gev"):
            return self.finish()
        self.phase_x1h2()
        if self.stage == "x1h2":
            return self.finish()
        self.phase_select()
        if self.stage == "select":
            return self.finish()
        self.phase_ffn()
        if self.stage == "ffn":
            return self.finish()
        self.phase_out()
        return self.finish()

    def finish(self):
        self.p.barrier()
        with self.nc.Block() as block:
            self.p.emit(block)
        while self._scopes:
            self.pop()
        for cm in reversed(self._persist):
            cm.__exit__(None, None, None)
        return

    def declare_io(self):
        self.din("x_in", (T, D))
        self.din("x_own", (TH, D))
        self.din("oidx_pack", (128, 24), I32)
        self.din("wrrep", (128, D))
        self.din("wq_t", (D, NHC * HD)); self.din("wk_t", (D, NHC * HD)); self.din("wv_t", (D, NHC * HD))
        self.din("wo_t", (D, D))
        self.din("lw_t", (D, NHC)); self.din("lb_row", (1, NHC))
        self.din("lnw_fm", (128, 8)); self.din("lnb_fm", (128, 8))
        self.din("lnw_tm", (1, NHC * HD)); self.din("lnb_tm", (1, NHC * HD))
        self.din("lnw_rep", (128, 2, HD)); self.din("lnb_rep", (128, 2, HD))
        self.din("w1_fm", (128, 8 * HD)); self.din("b1_fm", (128, 8))
        self.din("f_mask", (SEL, SEL))
        self.din("f_freqs_c", (FHD // 2, SEL)); self.din("f_freqs_s", (FHD // 2, SEL))
        self.din("f_router", (128, DC)); self.din("f_g1", (128, DC)); self.din("f_g2", (128, DC))
        self.din("f_wq", (D, D)); self.din("f_wk", (D, D)); self.din("f_wv", (D, D)); self.din("f_wo", (D, D))
        self.din("f_w1", (D, FFN)); self.din("f_w3", (D, FFN)); self.din("f_w2", (FFN, D))
        self.din("c_fm_in", (128, DC, B))
        self.din("bsel_in", (B, 1))
        self.din("wmod_sl", (D, 1536))
        self.dout("out", (TH, D))
        self.dint("lnx_sp", (T, D), BF16)
        self.dint("xb_sp", (8, 128, T), BF16)
        self.dint("xc_sp", (8, 128, T), BF16)
        self.dint("t2_sp", (8, 128, T), BF16)
        self.dint("outs_ag_in", (8, 128, T), BF16)
        self.dint("oxc_in", (8, 128, TH), BF16)
        self.dint("oxc_out", (2, 8, 128, TH), BF16)
        self.dint("h2_sp", (TH, D), BF16)
        self.dint("lg_in", (128, 16))
        self.dint("lg_out", (2, 128, 16))
        self.dint("gcmp", (SEL + 8, D), BF16)
        self.dint("mod_ag_in", (B, 1536))
        self.dint("mod_ag_out", (8, B, 1536))
        self.dint("modsel_sp", (1, 6 * D))
        self.dint("coneg_sp", (T, NHC))
        self.dint("bo_ag_in", (SELH + 4, D), BF16)
        self.dint("bo_ag_out", (2, SELH + 4, D), BF16)

    def phase_consts(self):
        p = self.p
        ident = self.sb("ident", (128, 128), BF16)
        identf = self.sb("identf", (128, 128))
        trilinc = self.sb("trilinc", (128, 128), BF16)
        onesbd8 = self.sb("onesbd8", (128, 8), BF16)
        ones_row = self.sb("ones_row", (1, 512), BF16)
        ones_col = self.sb("ones_col", (128, 1), BF16)
        epsln = self.sb("epsln", (128, 1))
        epsttt = self.sb("epsttt", (128, 1))
        tokfneg = self.sb("tokfneg", (128, 1))
        self.push()
        rowi = self.sbt("rowi", (128, 128), I32)
        coli = self.sbt("coli", (128, 128), I32)
        rowf = self.sbt("rowf", (128, 128))
        colf = self.sbt("colf", (128, 128))
        tA = self.sbt("tA", (128, 8))
        tB = self.sbt("tB", (128, 8))
        tC = self.sbt("tC", (128, 8))
        tD = self.sbt("tD", (128, 8))
        p.opc("gpsimd", lambda g: g.iota(rowi[:], pattern=[[0, 128]], base=0, channel_multiplier=1))
        vg = p.opc("gpsimd", lambda g: g.iota(coli[:], pattern=[[1, 128]], base=0, channel_multiplier=0))
        p.wait("vector", "ser_gpsimd", vg)
        p.opc("vector", lambda v: v.tensor_copy(rowf[:], rowi[:]))
        p.opc("vector", lambda v: v.tensor_copy(colf[:], coli[:]))
        p.opc("vector", lambda v: v.tensor_tensor(ident[:], rowf[:], colf[:], AL.is_equal))
        p.opc("vector", lambda v: v.tensor_tensor(identf[:], rowf[:], colf[:], AL.is_equal))
        p.opc("vector", lambda v: v.tensor_tensor(trilinc[:], rowf[:], colf[:], AL.is_le))
        p.opc("vector", lambda v: v.tensor_scalar(tA[:], colf[:, 0:8], 16.0, None, AL.mult))
        p.opc("vector", lambda v: v.tensor_tensor(tB[:], rowf[:, 0:8], tA[:], AL.subtract))
        p.opc("vector", lambda v: v.tensor_scalar(tC[:], tB[:], -0.5, None, AL.is_ge))
        p.opc("vector", lambda v: v.tensor_scalar(tD[:], tB[:], 15.5, None, AL.is_le))
        p.opc("vector", lambda v: v.tensor_tensor(onesbd8[:], tC[:], tD[:], AL.mult))
        p.opc("vector", lambda v: v.memset(ones_row[:], 1.0))
        p.opc("vector", lambda v: v.memset(ones_col[:], 1.0))
        p.opc("vector", lambda v: v.memset(epsln[:], EPS_LN))
        p.opc("vector", lambda v: v.memset(epsttt[:], EPS_TTT))
        ki1 = self.sbt("ki1", (128, 1), I32)
        ki2 = self.sbt("ki2", (128, 1), I32)
        p.opc("vector", lambda v: v.tensor_scalar(ki1[:], rowi[:, 0:1], 4, None, AL.arith_shift_right))
        p.opc("vector", lambda v: v.tensor_scalar(ki1[:], ki1[:], 4, None, AL.arith_shift_left))
        p.opc("vector", lambda v: v.tensor_tensor(ki2[:], rowi[:, 0:1], ki1[:], AL.subtract))
        p.opc("vector", lambda v: v.tensor_copy(tokfneg[:], ki2[:]))
        p.opc("vector", lambda v: v.tensor_scalar_add(tokfneg[:], tokfneg[:], 1.0))
        p.opc("vector", lambda v: v.reciprocal(tokfneg[:], tokfneg[:]))
        p.opc("vector", lambda v: v.tensor_scalar(tokfneg[:], tokfneg[:], -1.0 / 64.0, None, AL.mult))
        p.barrier()
        self.pop()

    def phase_mod(self):
        p = self.p
        reps = self.sb("reps", (128, 4, D), BF16)
        safm = self.sb("safm", (128, DC))
        safm_bf = self.sb("safm_bf", (128, DC), BF16)
        cafm = self.sb("cafm", (128, DC))

        self.push()
        c_fm = self.sbt("c_fm", (128, DC, B))
        silu_fm = self.sbt("silu_fm", (128, DC, B), BF16)
        wm_sb = self.sbt("wm_sb", (128, DC, 1536), BF16)
        mod_ps = [self.pst(f"mod_ps{i}", (B, 512)) for i in range(3)]
        mod_sb = self.sbt("mod_sb", (B, 1536))
        bsel = self.sbt("bsel", (B, 1), BF16)

        p.op("gpsimd", lambda g: g.dma_start(out=c_fm[:], in_=self.dram["c_fm_in"][:]), "mod_ld", 16)
        p.op("gpsimd", lambda g: g.dma_start(
            out=wm_sb[:], in_=self.dram["wmod_sl"].rearrange("(kc kp) n -> kp kc n", kp=128)), "mod_ld", 16)
        p.op("gpsimd", lambda g: g.dma_start(out=bsel[:], in_=self.dram["bsel_in"][:]), "mod_ld", 16)
        p.wait("scalar", "mod_ld", p.count("mod_ld"))
        vs = p.opc("scalar", lambda s: s.activation(silu_fm[:], c_fm[:], AF.Silu))
        p.wait("tensor", "ser_scalar", vs)
        p.wait("tensor", "mod_ld", p.count("mod_ld"))
        for n in range(3):
            for kc in range(DC):
                p.op("tensor", lambda t, n=n, kc=kc: t.matmul(
                    mod_ps[n][:, :], silu_fm[:, kc, :], wm_sb[:, kc, n * 512:(n + 1) * 512],
                    start=(kc == 0), stop=(kc == DC - 1)), "mod_mm")
        p.wait("vector", "mod_mm", p.count("mod_mm"))
        for n in range(3):
            p.opc("vector", lambda vv, n=n: vv.tensor_copy(mod_sb[:, n * 512:(n + 1) * 512], mod_ps[n][:]))
        p.wait("gpsimd", "ser_vector", p.count("ser_vector"))
        v = p.op("gpsimd", lambda g: g.dma_start(out=self.dram["mod_ag_in"][:], in_=mod_sb[:]), "mod_st", 16)
        p.wait("gpsimd", "mod_st", v)
        v = p.op("gpsimd", lambda g: g.collective_compute(
            "AllGather", AL.bypass, replica_groups=RG,
            ins=[self.dram["mod_ag_in"][:]], outs=[self.dram["mod_ag_out"][:]]), "mod_cc")
        p.wait("gpsimd", "mod_cc", v)

        al_t = self.sbt("al_t", (B, 512))
        al_bf = self.sbt("al_bf", (B, 512), BF16)
        sel_ps = self.pst("sel_ps", (1, 512))
        sel_sb = self.sbt("sel_sb", (1, 512))
        msp = self.dram["modsel_sp"]
        for n in range(24):
            j, nn = n // 3, (n % 3) * 512
            v = p.op("gpsimd", lambda g, j=j, nn=nn: g.dma_start(
                out=al_t[:], in_=self.dram["mod_ag_out"][j, :, nn:nn + 512]), "msl_ld", 16)
            p.wait("vector", "msl_ld", v)
            vv = p.opc("vector", lambda vv: vv.tensor_copy(al_bf[:], al_t[:]))
            p.wait("tensor", "ser_vector", vv)
            v = p.op("tensor", lambda t: t.matmul(sel_ps[:, :], bsel[:, :], al_bf[:, :],
                                                  start=True, stop=True), "msl_mm")
            p.wait("vector", "msl_mm", v)
            vv = p.opc("vector", lambda vv: vv.tensor_copy(sel_sb[:], sel_ps[:]))
            p.wait("gpsimd", "ser_vector", vv)
            v = p.op("gpsimd", lambda g, n=n: g.dma_start(
                out=msp[0:1, n * 512:(n + 1) * 512], in_=sel_sb[:]), "msl_st", 16)
            p.wait("vector", "msl_st", v)
            p.wait("tensor", "msl_st", v)

        grow = self.sbt("grow", (1, 2, D))
        absg = self.sbt("absg", (1, 2, D))
        sgng = self.sbt("sgng", (1, 2, D))
        selrow_bf = self.sbt("selrow_bf", (1, 4, D), BF16)
        p.op("gpsimd", lambda g: g.dma_start(out=grow[:, 0, :], in_=msp[0:1, 2 * D:3 * D]), "clp_ld", 16)
        p.op("gpsimd", lambda g: g.dma_start(out=grow[:, 1, :], in_=msp[0:1, 5 * D:6 * D]), "clp_ld", 16)
        p.op("gpsimd", lambda g: g.dma_start(out=selrow_bf[:, 2, :], in_=msp[0:1, 4 * D:5 * D]), "clp_ld", 16)
        p.op("gpsimd", lambda g: g.dma_start(out=selrow_bf[:, 3, :], in_=msp[0:1, 3 * D:4 * D]), "clp_ld", 16)
        p.wait("scalar", "clp_ld", p.count("clp_ld"))
        p.opc("scalar", lambda sc: sc.activation(absg[:], grow[:], AF.Abs))
        vs = p.opc("scalar", lambda sc: sc.sign(sgng[:], grow[:]))
        p.wait("vector", "ser_scalar", vs)
        p.opc("vector", lambda vv: vv.tensor_scalar_max(absg[:], absg[:], 0.25))
        p.opc("vector", lambda vv: vv.tensor_tensor(selrow_bf[:, 0:2, :], absg[:], sgng[:], AL.mult))
        vv = p.opc("vector", lambda vv: vv.tensor_scalar_add(selrow_bf[:, 2, :], selrow_bf[:, 2, :], 1.0))
        p.wait("tensor", "ser_vector", vv)
        p.wait("tensor", "clp_ld", p.count("clp_ld"))

        rep_ps = self.pst("rep_ps", (128, 512))
        for r in range(4):
            for n in range(4):
                v = p.op("tensor", lambda t, r=r, n=n: t.matmul(
                    rep_ps[:, :], self.sb_h["ones_row"][:, 0:128], selrow_bf[:, r, n * 512:(n + 1) * 512],
                    start=True, stop=True), "rep_mm")
                p.wait("vector", "rep_mm", v)
                ev = p.opc("vector", lambda vv, r=r, n=n: vv.tensor_copy(
                    reps[:, r, n * 512:(n + 1) * 512], rep_ps[:]))
                p.wait("tensor", "ser_vector", ev)
        p.wait("gpsimd", "ser_vector", p.count("ser_vector"))

        p.op("gpsimd", lambda g: g.dma_start(
            out=safm[:], in_=msp[0, 0:D].rearrange("(kc kp) -> kp kc", kp=128)), "fm_ld", 16)
        v = p.op("gpsimd", lambda g: g.dma_start(
            out=cafm[:], in_=msp[0, D:2 * D].rearrange("(kc kp) -> kp kc", kp=128)), "fm_ld", 16)
        p.wait("vector", "fm_ld", p.count("fm_ld"))
        p.opc("vector", lambda vv: vv.tensor_scalar_add(cafm[:], cafm[:], 1.0))
        p.opc("vector", lambda vv: vv.tensor_copy(safm_bf[:], safm[:]))
        p.barrier()
        self.pop()
        self.tap_dram("modsel", "modsel_sp", (1, 6 * D))

    def phase_ln1(self):
        p = self.p
        self.push()
        x_t = self.sbt("x_t", (128, 2, D))
        sq_t = self.sbt("sq_t", (128, D))
        lnx_t = self.sbt("lnx_t", (128, 2, D), BF16)
        sums = self.sbt("sums", (128, 8))
        xin = self.dram["x_in"]
        lnsp = self.dram["lnx_sp"]
        eps = self.sb_h["epsln"]
        for i in range(T // 128):
            bi = i % 2
            if i >= 2:
                p.wait("gpsimd", "ln_st", 16 * (i - 1))
            v = p.op("gpsimd", lambda g, i=i, bi=bi: g.dma_start(
                out=x_t[:, bi, :], in_=xin[i * 128:(i + 1) * 128]), "ln_ldx", 16)
            p.wait("vector", "ln_ldx", v)
            p.wait("scalar", "ln_ldx", v)
            if i >= 1:
                p.wait("scalar", "ser_vector", self._ln_lastapply)
            vs = p.opc("scalar", lambda s, bi=bi: s.activation(
                sq_t[:], x_t[:, bi, :], AF.Square, accum_out=sums[:, 1:2]))
            p.opc("vector", lambda vv, bi=bi: vv.reduce_sum(
                sums[:, 0:1], x_t[:, bi, :], axis=mybir.AxisListType.X))
            p.opc("vector", lambda vv: vv.tensor_scalar(sums[:, 2:3], sums[:, 0:1], 1.0 / D, None, AL.mult))
            p.opc("vector", lambda vv: vv.tensor_tensor(sums[:, 3:4], sums[:, 2:3], sums[:, 2:3], AL.mult))
            p.wait("vector", "ser_scalar", vs)
            vv = p.opc("vector", lambda vv: vv.scalar_tensor_tensor(
                sums[:, 4:5], sums[:, 1:2], 1.0 / D, sums[:, 3:4], AL.mult, AL.subtract))
            p.wait("scalar", "ser_vector", vv)
            vs = p.opc("scalar", lambda s: s.activation(sums[:, 5:6], sums[:, 4:5], AF.Sqrt, bias=eps[:, :]))
            p.wait("vector", "ser_scalar", vs)
            p.opc("vector", lambda vv: vv.reciprocal(sums[:, 6:7], sums[:, 5:6]))
            if i >= 2:
                p.wait("vector", "ln_st", 16 * (i - 1))
            v = p.opc("vector", lambda vv, bi=bi: vv.tensor_scalar(
                lnx_t[:, bi, :], x_t[:, bi, :], sums[:, 2:3], sums[:, 6:7], AL.subtract, AL.mult))
            self._ln_lastapply = v
            p.wait("gpsimd", "ser_vector", v)
            p.op("gpsimd", lambda g, i=i, bi=bi: g.dma_start(
                out=lnsp[i * 128:(i + 1) * 128], in_=lnx_t[:, bi, :]), "ln_st", 16)
        p.wait("gpsimd", "ln_st", p.count("ln_st"))
        p.barrier()
        self.pop()
        self.tap_dram("lnx", "lnx_sp", (T, D), BF16)

    def phase_wprep(self):
        pass

    def phase_proj_scan(self):
        self.phase_proj()
        if self.stage == "proj":
            return
        self.phase_scan()

    def phase_proj(self):
        p = self.p
        c2w = self.sb("c2w", (128, 3, 8))
        c2mb = self.sb("c2mb", (128, 8))
        self.push()
        wt = self.sbt("wt", (128, DC, 1024), BF16)
        h1 = self.sbt("h1", (128, DC, 1024), BF16)
        evb = self.sbt("evb", (128, 2, 512), BF16)
        m1sb = self.sbt("m1sb", (128, 512))
        xbb = self.sbt("xbb", (128, 512), BF16)
        lwt = self.sbt("lwt", (128, DC, 16), BF16)
        biasrow = self.sbt("biasrow", (1, 16))
        biasbf = self.sbt("biasbf", (1, 16), BF16)
        sig = self.sbt("sig", (128, 16))
        cng = self.sbt("cng", (128, 16))
        c2ps = self.pst("c2ps", (128, 8))
        c2lw_ps = self.pst("c2lw_ps", (1, 16))
        pj_ps = [self.pst(f"pj{i}", (128, 512)) for i in range(2)]
        ilr_ps = self.pst("ilr_ps", (128, 16))
        lnw_fm = self.sbt("lnw_fm_s", (128, 8))
        lnb_fm = self.sbt("lnb_fm_s", (128, 8))
        p.op("gpsimd", lambda g: g.dma_start(out=lnw_fm[:], in_=self.dram["lnw_fm"][:]), "pj_cld", 16)
        p.op("gpsimd", lambda g: g.dma_start(out=lnb_fm[:], in_=self.dram["lnb_fm"][:]), "pj_cld", 16)
        p.op("gpsimd", lambda g: g.dma_start(out=biasrow[:], in_=self.dram["lb_row"][:]), "pj_cld", 16)
        specs = [("wk_t", "xb_sp"), ("wv_t", "t2_sp"), ("wq_t", "xc_sp")]
        evbst = {0: 0, 1: 0}
        for wi, (wname, spname) in enumerate(specs):
            if wi > 0:
                p.wait("gpsimd", "pj_mm_done", wi)
            v = p.op("gpsimd", lambda g, wname=wname: g.dma_start(
                out=wt[:], in_=self.dram[wname].rearrange("(kc kp) n -> kp kc n", kp=128)), "pj_wld", 16)
            p.wait("tensor", "pj_wld", v)
            if wi == 0:
                v = p.op("gpsimd", lambda g: g.dma_start(
                    out=lwt[:], in_=self.dram["lw_t"].rearrange("(kc kp) n -> kp kc n", kp=128)), "pj_wld2", 16)
                p.wait("tensor", "pj_wld2", v)
            for j in range(8):
                for kc in range(DC):
                    p.op("tensor", lambda t, j=j, kc=kc: t.matmul(
                        c2ps[:, j:j + 1], wt[:, kc, j * 128:(j + 1) * 128],
                        self.sb_h["safm_bf"][:, kc:kc + 1],
                        start=(kc == 0), stop=(kc == DC - 1)), "pj_c2mm")
            if wi == 0:
                for kc in range(DC):
                    p.op("tensor", lambda t, kc=kc: t.matmul(
                        c2lw_ps[:, :], self.sb_h["safm_bf"][:, kc:kc + 1], lwt[:, kc, :],
                        start=(kc == 0), stop=(kc == DC - 1)), "pj_c2mm")
            v = p.count("pj_c2mm")
            p.wait("vector", "pj_c2mm", v)
            vv = p.opc("vector", lambda vv, wi=wi: vv.tensor_copy(c2w[:, wi, :], c2ps[:]))
            if wi == 1:
                vv = p.opc("vector", lambda vv: vv.tensor_tensor(c2mb[:], c2w[:, 1, :], lnb_fm[:], AL.subtract))
            if wi == 0:
                p.wait("vector", "pj_cld", p.count("pj_cld"))
                p.opc("vector", lambda vv: vv.scalar_tensor_tensor(
                    biasrow[:], c2lw_ps[:], 1.0, biasrow[:], AL.mult, AL.add))
                vv = p.opc("vector", lambda vv: vv.tensor_copy(biasbf[:], biasrow[:]))
            for kc in range(DC):
                vv = p.opc("vector", lambda vv, kc=kc: vv.tensor_scalar(
                    wt[:, kc, :], wt[:, kc, :], self.sb_h["cafm"][:, kc:kc + 1], None, AL.mult))
                if wi == 0:
                    vv = p.opc("vector", lambda vv, kc=kc: vv.tensor_scalar(
                        lwt[:, kc, :], lwt[:, kc, :], self.sb_h["cafm"][:, kc:kc + 1], None, AL.mult))
            p.wait("tensor", "ser_vector", vv)

            if wi == 1:
                p.wait("gpsimd", "pj_st", self._pj_st_w0)
            for kb in range(NBLK):
                p.wait("sync", "pj_blk", p.count("pj_blk"))
                for tt in range(8):
                    row = kb * BLKT + tt * 128
                    v = p.op("sync", lambda sn, row=row, tt=tt: sn.dma_start(
                        out=h1[:, :, tt * 128:(tt + 1) * 128],
                        in_=self.dram["lnx_sp"][row:row + 128], transpose=True), "pj_tr", 16)
                    p.wait("sync", "pj_tr", v)
                p.wait("tensor", "pj_tr", p.count("pj_tr"))
                for j in range(8):
                    for n in range(2):
                        pb = (j * 2 + n) % 2
                        if evbst[pb]:
                            p.wait("vector", "pj_st", evbst[pb])
                        for kc in range(DC):
                            p.op("tensor", lambda t, j=j, n=n, kc=kc, pb=pb: t.matmul(
                                pj_ps[pb][:, :], wt[:, kc, j * 128:(j + 1) * 128],
                                h1[:, kc, n * 512:(n + 1) * 512],
                                start=(kc == 0), stop=(kc == DC - 1)), "pj_mm")
                        v = p.count("pj_mm")
                        p.wait("vector", "pj_mm", v)
                        col = kb * BLKT + n * 512
                        if wi != 1:
                            vv = p.opc("vector", lambda vv, j=j, pb=pb, wi=wi: vv.tensor_scalar(
                                evb[:, pb, :], pj_ps[pb][:], c2w[:, wi, j:j + 1], None, AL.add))
                        else:
                            vld = p.op("gpsimd", lambda g, j=j, col=col: g.dma_start(
                                out=xbb[:], in_=self.dram["xb_sp"][j, :, col:col + 512]), "pj_xbld", 16)
                            p.wait("vector", "pj_xbld", vld)
                            p.opc("vector", lambda vv, j=j, pb=pb: vv.tensor_scalar(
                                m1sb[:], pj_ps[pb][:], c2mb[:, j:j + 1], None, AL.add))
                            p.opc("vector", lambda vv: vv.tensor_tensor(m1sb[:], xbb[:], m1sb[:], AL.subtract))
                            vv = p.opc("vector", lambda vv, j=j, pb=pb: vv.tensor_scalar(
                                evb[:, pb, :], m1sb[:], lnw_fm[:, j:j + 1], None, AL.mult))
                        p.wait("tensor", "ser_vector", vv)
                        p.wait("gpsimd", "ser_vector", vv)
                        evbst[pb] = p.op("gpsimd", lambda g, j=j, col=col, pb=pb, spname=spname: g.dma_start(
                            out=self.dram[spname][j, :, col:col + 512], in_=evb[:, pb, :]), "pj_st", 16)
                if wi == 0:
                    for tt in range(8):
                        for kc in range(DC):
                            p.op("tensor", lambda t, tt=tt, kc=kc: t.matmul(
                                ilr_ps[:, :], h1[:, kc, tt * 128:(tt + 1) * 128], lwt[:, kc, :],
                                start=(kc == 0), stop=False), "pj_imm")
                        v = p.op("tensor", lambda t: t.matmul(
                            ilr_ps[:, :], self.sb_h["ones_row"][0:1, 0:128], biasbf[:, :],
                            start=False, stop=True), "pj_imm")
                        p.wait("scalar", "pj_imm", v)
                        vs = p.opc("scalar", lambda s: s.activation(sig[:], ilr_ps[:], AF.Sigmoid))
                        p.wait("vector", "ser_scalar", vs)
                        vv = p.opc("vector", lambda vv: vv.tensor_scalar(
                            cng[:], sig[:], self.sb_h["tokfneg"][:, :], None, AL.mult))
                        p.wait("gpsimd", "ser_vector", vv)
                        row = kb * BLKT + tt * 128
                        v = p.op("gpsimd", lambda g, row=row: g.dma_start(
                            out=self.dram["coneg_sp"][row:row + 128], in_=cng[:]), "pj_cst", 16)
                        p.wait("tensor", "pj_cst", v)
                p.op("tensor", lambda t: t.nop(), "pj_blk", 1)
            if wi == 0:
                self._pj_st_w0 = p.count("pj_st")
            p.op("tensor", lambda t: t.nop(), "pj_mm_done", 1)
        p.wait("gpsimd", "pj_st", p.count("pj_st"))
        p.wait("gpsimd", "pj_cst", p.count("pj_cst"))
        p.barrier()
        self.pop()
        self.tap_dram("xb", "xb_sp", (8, 128, T), BF16)
        self.tap_dram("xc", "xc_sp", (8, 128, T), BF16)
        self.tap_dram("t2", "t2_sp", (8, 128, T), BF16)
        self.tap_dram("coneg", "coneg_sp", (T, NHC))

    def phase_scan(self):
        p = self.p
        self.push()
        xbw = self.sbt("xbw", (128, 8, 128), BF16)
        xcw = self.sbt("xcw", (128, 8, 128), BF16)
        t2w = self.sbt("t2w", (128, 8, 128), BF16)
        cngtm = self.sbt("cngtm", (128, 16))
        cngb = self.sbt("cngb", (128, 8, 2))
        w1bf = self.sbt("w1bf", (128, 8 * HD), BF16)
        b1fm = self.sbt("b1fm", (128, 8))
        b1run = self.sbt("b1run", (128, 8))
        b1c = self.sbt("b1c", (128, 8))
        w1bd = self.sbt("w1bd", (128, 8, 128), BF16)
        g2rep = self.sbt("g2rep", (128, 2, HD))
        lnwrep = self.sbt("lnwrep", (128, 2, HD))
        lnbrep = self.sbt("lnbrep", (128, 2, HD))
        gtmrow = self.sbt("gtmrow", (1, 2, NHC * HD), BF16)
        grepTM = self.sbt("grepTM", (128, NHC * HD))
        brepTM = self.sbt("brepTM", (128, NHC * HD))
        xbtm = self.sbt("xbtm", (128, 1024), BF16)
        xbwl = self.sbt("xbwl", (128, 8, 128), BF16)
        xbwh = self.sbt("xbwh", (128, 8, 128), BF16)
        xbcxt = self.sbt("xbcxt", (128, 16, 128), BF16)
        xccxt = self.sbt("xccxt", (128, 16, 128), BF16)
        gradtm = self.sbt("gradtm", (128, 1024), BF16)
        cogtm = self.sbt("cogtm", (128, 1024), BF16)
        gfmslab = self.sbt("gfmslab", (128, 8, 128))
        z1c = self.sbt("z1c", (128, 128))
        t2c = self.sbt("t2c", (128, 128))
        z1tsb = self.sbt("z1tsb", (128, 2, HD))
        xh = self.sbt("xh", (128, 2, HD))
        t7 = self.sbt("t7", (128, 2, HD))
        gg = self.sbt("gg", (128, 2, HD))
        t3 = self.sbt("t3", (128, 2, HD))
        gradb = self.sbt("gradb", (128, 128))
        cogrb = self.sbt("cogrb", (128, 128), BF16)
        sqscr = self.sbt("sqscr", (128, 2, HD))
        stats = self.sbt("stats", (128, 24))
        zbar = self.sbt("zbar", (128, 1024))
        zbtm = self.sbt("zbtm", (128, 1024))
        sq2 = self.sbt("sq2", (128, 1024))
        stz = self.sbt("stz", (128, 16))
        stq = self.sbt("stq", (128, 16))
        wst = self.sbt("wst", (128, 16, 4))
        outw = self.sbt("outw", (128, 8, 128), BF16)
        w1ps = self.pst("w1ps", (128, 8 * HD))
        z1win = self.pst("z1win", (128, 1024))
        zbwin = self.pst("zbwin", (128, 1024))
        scrps = self.pst("scrps", (128, 1024))
        scr = scrps[:, 0:256]
        slot = scrps[:, 256:768]
        gramps = scrps
        ident = self.sb_h["ident"]
        identf = self.sb_h["identf"]
        tril = self.sb_h["trilinc"]
        ob8 = self.sb_h["onesbd8"]
        eps = self.sb_h["epsttt"]
        SZ, SQ, MU, MS, VA, SD, RS, SG, SGX, NC2, NC1 = 0, 2, 4, 6, 8, 10, 12, 16, 18, 20, 22

        def TW(eng, sem, val):
            p.wait(eng, sem, val)

        p.op("gpsimd", lambda g: g.dma_start(out=w1bf[:], in_=self.dram["w1_fm"][:]), "sc_ld0", 16)
        p.op("gpsimd", lambda g: g.dma_start(out=b1fm[:], in_=self.dram["b1_fm"][:]), "sc_ld0", 16)
        p.op("gpsimd", lambda g: g.dma_start(out=lnwrep[:], in_=self.dram["lnw_rep"][:]), "sc_ld0", 16)
        p.op("gpsimd", lambda g: g.dma_start(out=lnbrep[:], in_=self.dram["lnb_rep"][:]), "sc_ld0", 16)
        p.op("gpsimd", lambda g: g.dma_start(out=gtmrow[:, 0, :], in_=self.dram["lnw_tm"][:]), "sc_ld0", 16)
        v = p.op("gpsimd", lambda g: g.dma_start(out=gtmrow[:, 1, :], in_=self.dram["lnb_tm"][:]), "sc_ld0", 16)
        p.wait("vector", "sc_ld0", p.count("sc_ld0"))
        p.opc("vector", lambda vv: vv.tensor_tensor(g2rep[:], lnwrep[:], lnwrep[:], AL.mult))
        vz = p.opc("vector", lambda vv: vv.memset(w1bd[:], 0.0))
        p.wait("tensor", "ser_vector", vz)
        p.wait("tensor", "sc_ld0", p.count("sc_ld0"))
        p.op("tensor", lambda t: t.matmul(w1ps[:, :], ident[:, :], w1bf[:, :], start=True, stop=True), "sc_mm")
        for r, dst in ((0, grepTM), (1, brepTM)):
            for n in range(2):
                v = p.op("tensor", lambda t, r=r, n=n: t.matmul(
                    gramps[:, 0:512], self.sb_h["ones_row"][0:1, 0:128],
                    gtmrow[:, r, n * 512:(n + 1) * 512], start=True, stop=True), "sc_mm")
                p.wait("vector", "sc_mm", v)
                vv = p.opc("vector", lambda vv, dst=dst, n=n: vv.tensor_copy(
                    dst[:, n * 512:(n + 1) * 512], gramps[:, 0:512]))
                p.wait("tensor", "ser_vector", vv)

        nwin = T // WND
        if getattr(self, "nwin_limit", None):
            nwin = self.nwin_limit
        for w in range(nwin):
            t0 = w * WND
            p.wait("gpsimd", "sc_win_done", max(0, p.count("sc_win_done")))
            for nm, buf in (("xb_sp", xbw), ("xc_sp", xcw), ("t2_sp", t2w)):
                p.op("gpsimd", lambda g, nm=nm, buf=buf, t0=t0: g.dma_start(
                    out=buf[:], in_=self.dram[nm][:, :, t0:t0 + WND].rearrange("j p t -> p j t")), "sc_ld", 16)
            p.op("gpsimd", lambda g, t0=t0: g.dma_start(
                out=cngtm[:], in_=self.dram["coneg_sp"][t0:t0 + WND]), "sc_ld", 16)
            for u in range(8):
                v = p.op("gpsimd", lambda g, t0=t0, u=u: g.dma_start(
                    out=cngb[16 * u:16 * u + 16, :, :],
                    in_=self.dram["coneg_sp"].flatten()[t0 * 16:(t0 + WND) * 16]
                    .rearrange("(c k r) -> c k r", c=8, k=16)[:, :, 2 * u:2 * u + 2]
                    .transpose([1, 0, 2])), "sc_ld", 16)
            ldv = p.count("sc_ld")
            p.wait("vector", "sc_ld", ldv)
            p.wait("tensor", "sc_ld", ldv)
            if self.stage == "scan_ld":
                continue
            p.wait("vector", "sc_mm", p.count("sc_mm"))
            p.opc("vector", lambda vv: vv.tensor_copy(
                w1bd[0:64, :, 0:64], w1ps[0:64, :].rearrange("q (u g) -> q u g", u=8)))
            p.opc("vector", lambda vv: vv.tensor_copy(
                w1bd[64:128, :, 64:128], w1ps[64:128, :].rearrange("q (u g) -> q u g", u=8)))
            p.opc("vector", lambda vv: vv.memset(b1run[:], 0.0))
            vv = p.opc("vector", lambda vv: vv.tensor_copy(b1c[:], b1fm[:]))
            p.wait("tensor", "ser_vector", vv)
            if self.stage == "scan_bd":
                continue
            for u in range(8):
                v = p.op("tensor", lambda t, u=u: t.transpose(
                    scr[:, 0:64].bitcast(BF16), xbw[:, u, :], ident[:, :]), "sc_mm")
                p.wait("vector", "sc_mm", v)
                vv = p.opc("vector", lambda vv, u=u: vv.tensor_copy(
                    xbtm[:, u * 128:(u + 1) * 128], scr[:, 0:64].bitcast(BF16)))
                p.wait("tensor", "ser_vector", vv)
            if self.stage == "scan_xbtm":
                continue
            p.opc("vector", lambda vv: vv.memset(xbwl[64:128, :, :], 0.0))
            p.opc("vector", lambda vv: vv.memset(xbwh[0:64, :, :], 0.0))
            p.opc("vector", lambda vv: vv.tensor_copy(xbwl[0:64, :, :], xbw[0:64, :, :]))
            vv = p.opc("vector", lambda vv: vv.tensor_copy(xbwh[64:128, :, :], xbw[64:128, :, :]))
            p.wait("tensor", "ser_vector", vv)
            for grp in range(4):
                for hh in range(4):
                    h = grp * 4 + hh
                    u, h2 = h // 2, h % 2
                    xm = xbwl if h2 == 0 else xbwh
                    p.op("tensor", lambda t, u=u, xm=xm, hh=hh: t.matmul(
                        gramps[:, hh * 256:hh * 256 + 128], xm[:, u, :], xbw[:, u, :],
                        start=True, stop=True), "sc_mm")
                    v = p.op("tensor", lambda t, u=u, xm=xm, hh=hh: t.matmul(
                        gramps[:, hh * 256 + 128:hh * 256 + 256], xm[:, u, :], xcw[:, u, :],
                        start=True, stop=True), "sc_mm")
                p.wait("vector", "sc_mm", v)
                cslice = slice(grp * 4, grp * 4 + 4)
                if self.stage == "scan_gmm":
                    continue
                p.opc("vector", lambda vv, grp=grp, cslice=cslice: vv.tensor_tensor(
                    xbcxt[:, cslice, :],
                    gramps[:].rearrange("p (hh q) -> p hh q", hh=4)[:, :, 0:128],
                    cngtm[:, cslice].unsqueeze(2).broadcast_to([128, 4, 128]), AL.mult))
                p.opc("vector", lambda vv, grp=grp, cslice=cslice: vv.tensor_tensor(
                    xccxt[:, cslice, :],
                    gramps[:].rearrange("p (hh q) -> p hh q", hh=4)[:, :, 128:256],
                    cngtm[:, cslice].unsqueeze(2).broadcast_to([128, 4, 128]), AL.mult))
                if self.stage != "scan_gev":
                    vv = p.opc("vector", lambda vv, cslice=cslice: vv.tensor_tensor(
                        xccxt[:, cslice, :], xccxt[:, cslice, :],
                        tril[:].unsqueeze(1).broadcast_to([128, 4, 128]), AL.mult))
                p.wait("tensor", "ser_vector", vv)
            if self.stage in ("scan_gram", "scan_gmm", "scan_gev"):
                continue
            for u in range(8):
                p.op("tensor", lambda t, u=u: t.matmul(
                    z1win[:, u * 128:(u + 1) * 128], w1bd[:, u, :], xbw[:, u, :],
                    start=(u % 4 == 0), stop=True), "sc_mm")
                p.op("tensor", lambda t, u=u: t.matmul(
                    zbwin[:, u * 128:(u + 1) * 128], w1bd[:, u, :], xcw[:, u, :],
                    start=(u % 4 == 0), stop=True), "sc_mm")
            if self.stage == "scan_pre":
                continue
            for c in range(CPW):
                vcell = c // 2
                if c > 0:
                    for h in range(NHC):
                        u, h2 = h // 2, h % 2
                        p.op("tensor", lambda t, h=h, u=u, h2=h2, c=c: t.matmul(
                            z1win[64 * h2:64 * h2 + 64, u * 128 + 16 * c:u * 128 + 16 * c + 16],
                            gradtm[0:16 * c, h * 64:(h + 1) * 64],
                            xbcxt[0:16 * c, h, 16 * c:16 * c + 16],
                            start=False, stop=True), "sc_mm")
                v = p.count("sc_mm")
                p.wait("vector", "sc_mm", v)
                p.opc("vector", lambda vv, c=c: vv.scalar_tensor_tensor(
                    z1c[:].rearrange("p (u k) -> p u k", u=8),
                    z1win[:].rearrange("p (u t) -> p u t", u=8)[:, :, 16 * c:16 * c + 16],
                    1.0, b1c[:].unsqueeze(2).broadcast_to([128, 8, 16]), AL.mult, AL.add))
                vv = p.opc("vector", lambda vv, c=c: vv.tensor_copy(
                    t2c[:].rearrange("p (u k) -> p u k", u=8),
                    t2w[:, :, 16 * c:16 * c + 16]))
                p.wait("tensor", "ser_vector", vv)
                p.op("tensor", lambda t: t.transpose(scr[:, 0:128], z1c[:], identf[:, :]), "sc_mm")
                v = p.op("tensor", lambda t: t.transpose(scr[:, 128:256], t2c[:], identf[:, :]), "sc_mm")
                p.wait("vector", "sc_mm", v)
                A = scr[:, 0:128].rearrange("p (h2 g) -> p h2 g", h2=2)
                Bp = scr[:, 128:256].rearrange("p (h2 g) -> p h2 g", h2=2)
                for h2 in range(2):
                    p.opc("vector", lambda vv, h2=h2: vv.tensor_scalar(
                        z1tsb[:, h2, :], A[:, h2, :], 1.0, 0.0, AL.mult, AL.add,
                        accum_out=stats[:, SZ + h2:SZ + h2 + 1]))
                    p.opc("vector", lambda vv, h2=h2: vv.scalar_tensor_tensor(
                        sqscr[:, h2, :], z1tsb[:, h2, :], 1.0, z1tsb[:, h2, :], AL.mult, AL.mult,
                        accum_out=stats[:, SQ + h2:SQ + h2 + 1]))
                p.opc("vector", lambda vv: vv.tensor_scalar(
                    stats[:, MU:MU + 2], stats[:, SZ:SZ + 2], 1.0 / HD, None, AL.mult))
                p.opc("vector", lambda vv: vv.tensor_tensor(
                    stats[:, MS:MS + 2], stats[:, MU:MU + 2], stats[:, MU:MU + 2], AL.mult))
                vv = p.opc("vector", lambda vv: vv.scalar_tensor_tensor(
                    stats[:, VA:VA + 2], stats[:, SQ:SQ + 2], 1.0 / HD, stats[:, MS:MS + 2],
                    AL.mult, AL.subtract))
                p.wait("scalar", "ser_vector", vv)
                vs = p.opc("scalar", lambda s: s.activation(
                    stats[:, SD:SD + 2], stats[:, VA:VA + 2], AF.Sqrt, bias=eps[:, :]))
                p.wait("vector", "ser_scalar", vs)
                p.opc("vector", lambda vv: vv.reciprocal(stats[:, RS:RS + 2], stats[:, SD:SD + 2]))
                for h2 in range(2):
                    p.opc("vector", lambda vv, h2=h2: vv.tensor_scalar(
                        xh[:, h2, :], z1tsb[:, h2, :], stats[:, MU + h2:MU + h2 + 1],
                        stats[:, RS + h2:RS + h2 + 1], AL.subtract, AL.mult))
                p.opc("vector", lambda vv: vv.tensor_tensor(t7[:], xh[:], g2rep[:], AL.mult))
                for h2 in range(2):
                    p.opc("vector", lambda vv, h2=h2: vv.scalar_tensor_tensor(
                        gg[:, h2, :], t7[:, h2, :], 1.0, Bp[:, h2, :], AL.mult, AL.add,
                        accum_out=stats[:, SG + h2:SG + h2 + 1]))
                for h2 in range(2):
                    p.opc("vector", lambda vv, h2=h2: vv.scalar_tensor_tensor(
                        sqscr[:, h2, :], gg[:, h2, :], 1.0, xh[:, h2, :], AL.mult, AL.mult,
                        accum_out=stats[:, SGX + h2:SGX + h2 + 1]))
                p.opc("vector", lambda vv: vv.tensor_scalar(
                    stats[:, NC2:NC2 + 4], stats[:, SG:SG + 4], -1.0 / HD, None, AL.mult))
                for h2 in range(2):
                    p.opc("vector", lambda vv, h2=h2: vv.scalar_tensor_tensor(
                        t3[:, h2, :], xh[:, h2, :], stats[:, NC1 + h2:NC1 + h2 + 1], gg[:, h2, :],
                        AL.mult, AL.add))
                for h2 in range(2):
                    p.opc("vector", lambda vv, h2=h2: vv.tensor_scalar(
                        gradb[:].rearrange("p (h2 g) -> p h2 g", h2=2)[:, h2, :],
                        t3[:, h2, :], stats[:, NC2 + h2:NC2 + h2 + 1],
                        stats[:, RS + h2:RS + h2 + 1], AL.add, AL.mult))
                vv = p.opc("vector", lambda vv, c=c: vv.tensor_tensor(
                    cogrb[:].rearrange("p (h2 g) -> p h2 g", h2=2), 
                    gradb[:].rearrange("p (h2 g) -> p h2 g", h2=2),
                    cngb[:, c, :].unsqueeze(2).broadcast_to([128, 2, HD]), AL.mult))
                p.wait("tensor", "ser_vector", vv)
                if self.stage == "scan_chain":
                    continue
                v = p.op("tensor", lambda t: t.matmul(
                    scr[:, 0:8], cogrb[:, :], ob8[:, :], start=True, stop=True), "sc_mm")
                p.wait("vector", "sc_mm", v)
                p.opc("vector", lambda vv: vv.tensor_tensor(b1run[:], b1run[:], scr[:, 0:8], AL.add))
                vv = p.opc("vector", lambda vv: vv.tensor_tensor(b1c[:], b1fm[:], b1run[:], AL.add))
                p.wait("tensor", "ser_vector", vv)
                v = p.op("tensor", lambda t: t.transpose(scr[:, 128:256], gradb[:], identf[:, :]), "sc_mm")
                p.wait("vector", "sc_mm", v)
                vv = p.opc("vector", lambda vv, c=c: vv.tensor_copy(
                    gfmslab[:, :, 16 * c:16 * c + 16],
                    scr[:, 128:256].rearrange("p (u k) -> p u k", u=8)))
                p.wait("tensor", "ser_vector", vv)
                rv = slice(32 * vcell, 32 * vcell + 32)
                pfx = 16 * (c + 1)
                for r in range(2):
                    for uu in range(4):
                        u = r * 4 + uu
                        p.op("tensor", lambda t, u=u, uu=uu, pfx=pfx: t.transpose(
                            slot[0:pfx, 128 * uu:128 * uu + 128],
                            gfmslab[:, u, 0:pfx], identf[:, :]), "sc_mm")
                    v = p.count("sc_mm")
                    p.wait("vector", "sc_mm", v)
                    p.opc("vector", lambda vv, r=r, rv=rv: vv.tensor_copy(
                        gradtm[rv, 512 * r:512 * r + 512], slot[rv, :]))
                    vv = p.opc("vector", lambda vv, r=r, rv=rv: vv.tensor_tensor(
                        cogtm[rv, 512 * r:512 * r + 512].rearrange("p (a g) -> p a g", a=8),
                        slot[rv, :].rearrange("p (a g) -> p a g", a=8),
                        cngtm[rv, 8 * r:8 * r + 8].unsqueeze(2).broadcast_to([32, 8, HD]), AL.mult))
                    p.wait("tensor", "ser_vector", vv)
            if self.stage == "scan_chunks":
                continue
            for h in range(NHC):
                u, h2 = h // 2, h % 2
                p.op("tensor", lambda t, h=h, u=u, h2=h2: t.matmul(
                    w1ps[64 * h2:64 * h2 + 64, 64 * u:64 * u + 64],
                    xbtm[:, h * 64:(h + 1) * 64], cogtm[:, h * 64:(h + 1) * 64],
                    start=False, stop=True), "sc_mm")
                p.op("tensor", lambda t, h=h, u=u, h2=h2: t.matmul(
                    zbwin[64 * h2:64 * h2 + 64, u * 128:(u + 1) * 128],
                    gradtm[:, h * 64:(h + 1) * 64], xccxt[:, h, :],
                    start=False, stop=True), "sc_mm")
            for u in range(8):
                p.op("tensor", lambda t, u=u: t.matmul(
                    zbwin[:, u * 128:(u + 1) * 128], cogtm[:, u * 128:(u + 1) * 128],
                    tril[:, :], start=False, stop=True), "sc_mm")
            v = p.count("sc_mm")
            p.wait("vector", "sc_mm", v)
            p.opc("vector", lambda vv: vv.scalar_tensor_tensor(
                zbar[:].rearrange("p (u t) -> p u t", u=8), 
                zbwin[:].rearrange("p (u t) -> p u t", u=8), 1.0,
                b1fm[:].unsqueeze(2).broadcast_to([128, 8, WND]), AL.mult, AL.add))
            vv = p.opc("vector", lambda vv: vv.tensor_tensor(b1fm[:], b1fm[:], b1run[:], AL.add))
            p.wait("tensor", "ser_vector", vv)
            for u in range(8):
                v = p.op("tensor", lambda t, u=u: t.transpose(
                    scr[:, 0:128], zbar[:, u * 128:(u + 1) * 128], identf[:, :]), "sc_mm")
                p.wait("vector", "sc_mm", v)
                vv = p.opc("vector", lambda vv, u=u: vv.tensor_copy(
                    zbtm[:, u * 128:(u + 1) * 128], scr[:, 0:128]))
                p.wait("tensor", "ser_vector", vv)
            p.wait("scalar", "ser_vector", p.count("ser_vector"))
            vs = p.opc("scalar", lambda s: s.activation(sq2[:], zbtm[:], AF.Square))
            p.opc("vector", lambda vv: vv.reduce_sum(
                stz[:], zbtm[:].rearrange("p (h g) -> p h g", h=16), axis=mybir.AxisListType.X))
            p.wait("vector", "ser_scalar", vs)
            p.opc("vector", lambda vv: vv.reduce_sum(
                stq[:], sq2[:].rearrange("p (h g) -> p h g", h=16), axis=mybir.AxisListType.X))
            p.opc("vector", lambda vv: vv.tensor_scalar(wst[:, :, 0], stz[:], 1.0 / HD, None, AL.mult))
            p.opc("vector", lambda vv: vv.tensor_tensor(wst[:, :, 1], wst[:, :, 0], wst[:, :, 0], AL.mult))
            vv = p.opc("vector", lambda vv: vv.scalar_tensor_tensor(
                wst[:, :, 2], stq[:], 1.0 / HD, wst[:, :, 1], AL.mult, AL.subtract))
            p.wait("scalar", "ser_vector", vv)
            vs = p.opc("scalar", lambda s: s.activation(wst[:, :, 3], wst[:, :, 2], AF.Sqrt, bias=eps[:, :]))
            p.wait("vector", "ser_scalar", vs)
            p.opc("vector", lambda vv: vv.reciprocal(wst[:, :, 3], wst[:, :, 3]))
            p.opc("vector", lambda vv: vv.tensor_tensor(
                zbtm[:].rearrange("p (h g) -> p h g", h=16), zbtm[:].rearrange("p (h g) -> p h g", h=16),
                wst[:, :, 0].unsqueeze(2).broadcast_to([128, 16, HD]), AL.subtract))
            p.opc("vector", lambda vv: vv.tensor_tensor(
                zbtm[:].rearrange("p (h g) -> p h g", h=16), zbtm[:].rearrange("p (h g) -> p h g", h=16),
                wst[:, :, 3].unsqueeze(2).broadcast_to([128, 16, HD]), AL.mult))
            p.opc("vector", lambda vv: vv.tensor_tensor(zbtm[:], zbtm[:], grepTM[:], AL.mult))
            vv = p.opc("vector", lambda vv: vv.tensor_tensor(zbtm[:], zbtm[:], brepTM[:], AL.add))
            p.wait("tensor", "ser_vector", vv)
            for u in range(8):
                v = p.op("tensor", lambda t, u=u: t.transpose(
                    scr[:, 0:128], zbtm[:, u * 128:(u + 1) * 128], identf[:, :]), "sc_mm")
                p.wait("vector", "sc_mm", v)
                vv = p.opc("vector", lambda vv, u=u: vv.tensor_tensor(
                    outw[:, u, :], scr[:, 0:128], xcw[:, u, :], AL.add))
                p.wait("tensor", "ser_vector", vv)
            p.wait("gpsimd", "ser_vector", vv)
            p.op("gpsimd", lambda g, t0=t0: g.dma_start(
                out=self.dram["outs_ag_in"][:, :, t0:t0 + WND].rearrange("j p t -> p j t"),
                in_=outw[:]), "sc_win_done", 16)
        p.wait("gpsimd", "sc_win_done", p.count("sc_win_done"))
        p.barrier()
        self.pop()
        self.tap_dram("outs", "outs_ag_in", (8, 128, T), BF16)

    def phase_x1h2(self):
        p = self.p
        lgts = self.sb("lgts", (128, 16))
        reps = self.sb_h["reps"]
        self.push()
        oidx = self.sbt("oidx", (128, 24), I32)
        wr = self.sbt("wr", (128, D))
        afm = self.sbt("afm", (128, 16, TH), BF16)
        p.op("gpsimd", lambda g: g.dma_start(out=oidx[:], in_=self.dram["oidx_pack"][:]), "xg_cld", 16)
        p.op("gpsimd", lambda g: g.dma_start(out=wr[:], in_=self.dram["wrrep"][:]), "xg_cld", 16)
        p.wait("gpsimd", "xg_cld", p.count("xg_cld"))
        oflat = self.dram["outs_ag_in"].flatten().rearrange("(r c) -> r c", c=TH)
        pflat = self.dram["oxc_out"].flatten().rearrange("(r c) -> r c", c=TH)

        self.push()
        gbuf = self.sbt("gbuf", (128, TH), BF16)
        for j in range(8):
            v = p.op("gpsimd", lambda g, j=j: g.indirect_dma_start(
                gbuf[:], None, oflat[:], IndirectOffsetOnAxis(oidx[:, 8 + j:9 + j], 0)), "xg_snd", 16)
            p.wait("gpsimd", "xg_snd", v)
            v = p.op("gpsimd", lambda g, j=j: g.dma_start(
                out=self.dram["oxc_in"][j], in_=gbuf[:]), "xg_snd", 16)
            p.wait("gpsimd", "xg_snd", v)
        vcc = p.op("gpsimd", lambda g: g.collective_compute(
            "AllGather", AL.bypass, replica_groups=RGP,
            ins=[self.dram["oxc_in"][:]], outs=[self.dram["oxc_out"][:]]), "xg_cc")
        self.pop()

        self.push()
        rhs = self.sbt("rhs", (128, 16, TH), BF16)
        wos = self.sbt("wos", (128, DC, 1024), BF16)
        for j in range(8):
            p.op("gpsimd", lambda g, j=j: g.indirect_dma_start(
                rhs[:, j, :], None, oflat[:], IndirectOffsetOnAxis(oidx[:, j:j + 1], 0)), "xg_rhs", 16)
        p.wait("gpsimd", "xg_cc", vcc)
        for j in range(8):
            p.op("gpsimd", lambda g, j=j: g.indirect_dma_start(
                rhs[:, 8 + j, :], None, pflat[:], IndirectOffsetOnAxis(oidx[:, 16 + j:17 + j], 0)), "xg_rhs", 16)
        p.wait("tensor", "xg_rhs", p.count("xg_rhs"))

        atps = [self.pst(f"atps{i}", (128, 512)) for i in range(2)]
        evs = {0: 0, 1: 0}
        woin = self.dram["wo_t"].rearrange("(kc kp) n -> kp kc n", kp=128)
        if "xwos" in self.taps:
            self.dint("xwos_sp", (2, 128, DC, 1024), BF16)
        for mh in range(2):
            if mh:
                p.wait("gpsimd", "ser_vector", max(evs.values()))
            v = p.op("gpsimd", lambda g, mh=mh: g.dma_start(
                out=wos[:], in_=woin[:, :, mh * 1024:(mh + 1) * 1024]), "xg_wld", 16)
            p.wait("tensor", "xg_wld", v)
            if "xwos" in self.taps:
                p.wait("gpsimd", "xg_wld", v)
                v2 = p.op("gpsimd", lambda g, mh=mh: g.dma_start(
                    out=self.dram["xwos_sp"][mh], in_=wos[:]), "xg_wdbg", 16)
                p.wait("gpsimd", "xg_wdbg", v2)
            for tc in range(4):
                for mm_ in range(8):
                    m = mh * 8 + mm_
                    pb = mm_ % 2
                    if evs[pb]:
                        p.wait("tensor", "ser_vector", evs[pb])
                    for kc in range(DC):
                        p.op("tensor", lambda t, mm_=mm_, kc=kc, tc=tc, pb=pb: t.matmul(
                            atps[pb][:, :], wos[:, kc, mm_ * 128:(mm_ + 1) * 128],
                            rhs[:, kc, tc * 512:(tc + 1) * 512],
                            start=(kc == 0), stop=(kc == DC - 1)), "xg_mm")
                    vm = p.count("xg_mm")
                    p.wait("vector", "xg_mm", vm)
                    evs[pb] = p.opc("vector", lambda vv, m=m, tc=tc, pb=pb: vv.tensor_copy(
                        afm[:, m, tc * 512:(tc + 1) * 512], atps[pb][:]))
        if "xrhs" in self.taps or "xafm" in self.taps:
            self.dint("xrhs_sp", (128, 16, TH), BF16)
            self.dint("xafm_sp", (128, 16, TH), BF16)
            p.wait("gpsimd", "ser_vector", max(evs.values()))
            v = p.op("gpsimd", lambda g: g.dma_start(out=self.dram["xrhs_sp"][:], in_=rhs[:]), "xdbg", 16)
            v = p.op("gpsimd", lambda g: g.dma_start(out=self.dram["xafm_sp"][:], in_=afm[:]), "xdbg", 16)
            p.wait("gpsimd", "xdbg", v)
        p.barrier()
        self.pop()
        self.tap_dram("xrhs", "xrhs_sp", (128, 16, TH), BF16)
        self.tap_dram("xafm", "xafm_sp", (128, 16, TH), BF16)
        self.tap_dram("xwos", "xwos_sp", (2, 128, DC, 1024), BF16)

        self.push()
        bufA = self.sbt("bufA", (128, D))
        bufB = self.sbt("bufB", (128, D))
        bufC = self.sbt("bufC", (128, D))
        attnT = self.sbt("attnT", (128, D), BF16)
        h2bf = self.sbt("h2bf", (128, D), BF16)
        st8 = self.sbt("st8", (128, 8))
        trps = self.pst("trps", (128, 128))
        eps = self.sb_h["epsln"]
        prev = {}
        for tt in range(16):
            if tt:
                p.wait("sync", "ser_gpsimd", prev["vg"])
            xv = p.op("sync", lambda s, tt=tt: s.dma_start(
                out=bufA[:], in_=self.dram["x_own"][tt * 128:(tt + 1) * 128]), "xg_xld", 16)
            for m in range(16):
                tv = p.op("tensor", lambda t, m=m, tt=tt: t.transpose(
                    trps[:, 0:64].bitcast(BF16), afm[:, m, tt * 128:(tt + 1) * 128],
                    self.sb_h["ident"][:, :]), "xg_tr")
                p.wait("vector", "xg_tr", tv)
                ev = p.opc("vector", lambda vv, m=m: vv.tensor_copy(
                    attnT[:, m * 128:(m + 1) * 128], trps[:, 0:64].bitcast(BF16)))
                p.wait("tensor", "ser_vector", ev)
            p.wait("vector", "xg_xld", xv)
            if tt:
                p.wait("vector", "ser_scalar", prev["vsc"])
                p.wait("vector", "ser_gpsimd", prev["vg1"])
            p.opc("vector", lambda vv: vv.tensor_tensor(bufB[:], attnT[:], reps[:, 0, :], AL.mult))
            va = p.opc("vector", lambda vv: vv.tensor_tensor(bufA[:], bufA[:], bufB[:], AL.add))
            p.wait("scalar", "ser_vector", va)
            vs = p.opc("scalar", lambda s: s.activation(
                bufB[:], bufA[:], AF.Square, accum_out=st8[:, 1:2]))
            p.opc("vector", lambda vv: vv.reduce_sum(st8[:, 0:1], bufA[:], axis=mybir.AxisListType.X))
            p.opc("vector", lambda vv: vv.tensor_scalar(st8[:, 2:3], st8[:, 0:1], 1.0 / D, None, AL.mult))
            p.opc("vector", lambda vv: vv.tensor_tensor(st8[:, 3:4], st8[:, 2:3], st8[:, 2:3], AL.mult))
            p.wait("vector", "ser_scalar", vs)
            vv_ = p.opc("vector", lambda vv: vv.scalar_tensor_tensor(
                st8[:, 4:5], st8[:, 1:2], 1.0 / D, st8[:, 3:4], AL.mult, AL.subtract))
            p.wait("scalar", "ser_vector", vv_)
            vs = p.opc("scalar", lambda s: s.activation(st8[:, 5:6], st8[:, 4:5], AF.Sqrt, bias=eps[:, :]))
            p.wait("vector", "ser_scalar", vs)
            p.opc("vector", lambda vv: vv.reciprocal(st8[:, 6:7], st8[:, 5:6]))
            p.opc("vector", lambda vv: vv.tensor_scalar(
                bufB[:], bufA[:], st8[:, 2:3], st8[:, 6:7], AL.subtract, AL.mult))
            p.opc("vector", lambda vv: vv.tensor_tensor(bufB[:], bufB[:], reps[:, 2, :], AL.mult))
            vh = p.opc("vector", lambda vv: vv.tensor_tensor(bufB[:], bufB[:], reps[:, 3, :], AL.add))
            if tt:
                p.wait("vector", "xg_ost", prev["ov"])
            vl = p.opc("vector", lambda vv, tt=tt: vv.scalar_tensor_tensor(
                bufC[:], bufB[:], 1.0, wr[:], AL.mult, AL.mult, accum_out=lgts[:, tt:tt + 1]))
            p.wait("scalar", "ser_vector", vh)
            if tt:
                p.wait("scalar", "xg_h2st", prev["stv"])
            vsc = p.opc("scalar", lambda s: s.activation(h2bf[:], bufB[:], AF.Copy))
            p.wait("sync", "ser_scalar", vsc)
            stv = p.op("sync", lambda s, tt=tt: s.dma_start(
                out=self.dram["h2_sp"][tt * 128:(tt + 1) * 128], in_=h2bf[:]), "xg_h2st", 16)
            p.wait("gpsimd", "ser_vector", vl)
            vg1 = p.opc("gpsimd", lambda g: g.tensor_tensor(bufC[:], bufB[:], reps[:, 1, :], AL.mult))
            vg = p.opc("gpsimd", lambda g: g.tensor_tensor(bufC[:], bufA[:], bufC[:], AL.add))
            p.wait("scalar", "ser_gpsimd", vg)
            ov = p.op("scalar", lambda s, tt=tt: s.dma_start(
                out=self.dram["out"][tt * 128:(tt + 1) * 128], in_=bufC[:]), "xg_ost", 16)
            prev = {"vg": vg, "vg1": vg1, "vsc": vsc, "stv": stv, "ov": ov}
        p.wait("gpsimd", "xg_ost", p.count("xg_ost"))
        p.wait("gpsimd", "xg_h2st", p.count("xg_h2st"))
        v = p.op("gpsimd", lambda g: g.dma_start(out=self.dram["lg_in"][:], in_=lgts[:]), "xg_lst", 16)
        p.wait("gpsimd", "xg_lst", v)
        v = p.op("gpsimd", lambda g: g.collective_compute(
            "AllGather", AL.bypass, replica_groups=RGP,
            ins=[self.dram["lg_in"][:]], outs=[self.dram["lg_out"][:]]), "xg_lcc")
        p.wait("gpsimd", "xg_lcc", v)
        p.barrier()
        self.pop()
        self.tap_dram("h2", "h2_sp", (TH, D), BF16)
        self.tap_dram("lgin", "lg_in", (128, 16))

    def phase_select(self):
        pass

    def phase_ffn(self):
        pass

    def phase_out(self):
        pass
